# version 41
# baseline (speedup 1.0000x reference)
import numpy as np
import ml_dtypes
from contextlib import ExitStack
import concourse.bass as bass
import concourse.mybir as mybir
from concourse.bass_utils import run_bass_kernel_spmd

F32 = mybir.dt.float32
BF16 = mybir.dt.bfloat16
AF = mybir.ActivationFunctionType
ALU = mybir.AluOpType
BF = ml_dtypes.bfloat16

HID = 2048
INTER = 4096
SEQ = 8192
N_CORES = 8
T_CORE = SEQ // N_CORES
NSP = 2
TSP = 512
NJ = TSP // 128
KH = HID // 128
KI = INTER // 128
NSLOT = 12
NBANK = 4


class Ctr:
    def __init__(self, name):
        self.name = name
        self.v = 0
        self.h = None

    def inc(self, n=1):
        self.v += n
        return self.v


def build_program():
    nc = bass.Bass("TRN2", target_bir_lowering=False)
    X = nc.dram_tensor("X", [T_CORE, HID], F32, kind="ExternalInput")
    W1S = nc.dram_tensor("W1S", [128, 64 * 16 * 128], BF16, kind="ExternalInput")
    W2S = nc.dram_tensor("W2S", [128, 32 * 32 * 128], BF16, kind="ExternalInput")
    W3S = nc.dram_tensor("W3S", [128, 4 * 32 * 512], BF16, kind="ExternalInput")
    W4S = nc.dram_tensor("W4S", [128, 32 * 16 * 128], BF16, kind="ExternalInput")
    W5S = nc.dram_tensor("W5S", [128, 32 * 32 * 128], BF16, kind="ExternalInput")
    W6S = nc.dram_tensor("W6S", [128, 4 * 32 * 512], BF16, kind="ExternalInput")
    CONST = nc.dram_tensor("CONST", [128, 97], F32, kind="ExternalInput")
    XBF = nc.dram_tensor("XBF", [KH * T_CORE, 128], BF16, kind="ExternalInput")
    XTOK = nc.dram_tensor("XTOK", [TSP, HID], BF16, kind="ExternalInput")
    ONF = nc.dram_tensor("ONF", [1, 128], F32, kind="ExternalInput")
    IDF = nc.dram_tensor("IDF", [128, 128], F32, kind="ExternalInput")
    IDB = nc.dram_tensor("IDB", [128, 128], BF16, kind="ExternalInput")
    BB3 = nc.dram_tensor("BB3", [128, HID], F32, kind="ExternalInput")
    OUT = nc.dram_tensor("OUT", [T_CORE, HID], F32, kind="ExternalOutput")

    wdram = {1: W1S, 2: W2S, 3: W3S, 4: W4S, 5: W5S, 6: W6S}

    plan = {k: [] for k in ("sync", "gpsimd", "tensor", "vector", "scalar")}

    def op(eng, fn):
        plan[eng].append(fn)

    def wait(eng, ctr, val):
        if val > 0:
            op(eng, lambda e, s=ctr, v=val: e.wait_ge(s.h, v))

    def wait_guards(eng, guards):
        agg = {}
        for g in guards:
            if g is None:
                continue
            s, v = g
            agg[s] = max(agg.get(s, 0), v)
        for s, v in agg.items():
            wait(eng, s, v)

    sem_ws = [Ctr(f"ws{i}") for i in range(NSLOT)]
    sem_xt = [Ctr(f"xt{i}") for i in range(NSP * NJ)]
    sem_c = [Ctr(f"c{i}") for i in range(4)]
    sem_xb = [Ctr(f"xb{i}") for i in range(NSP)]
    sem_tk = [Ctr(f"tk{i}") for i in range(NJ)]
    sem_pe = Ctr("pe")
    sem_act = Ctr("act")
    sem_dve = Ctr("dve")
    sem_out = Ctr("out")
    sem_warm = Ctr("warm")

    with ExitStack() as ctx:
        sX = ctx.enter_context(nc.sbuf_tensor("sX", [128, NSP * NJ * HID], F32))
        sXN = ctx.enter_context(nc.sbuf_tensor("sXN", [128, KH * TSP], BF16))
        sRA = ctx.enter_context(nc.sbuf_tensor("sRA", [128, KI * TSP], BF16))
        sRB = ctx.enter_context(nc.sbuf_tensor("sRB", [128, KI * TSP], BF16))
        sWr = ctx.enter_context(nc.sbuf_tensor("sWr", [128, NSLOT * 2048], BF16))
        sST = ctx.enter_context(nc.sbuf_tensor("sST", [128, NSP * NJ * 32], F32))
        sCN = ctx.enter_context(nc.sbuf_tensor("sCN", [128, 97], F32))
        sONF = ctx.enter_context(nc.sbuf_tensor("sONF", [1, 128], F32))
        sIDF = ctx.enter_context(nc.sbuf_tensor("sIDF", [128, 128], F32))
        sIDB = ctx.enter_context(nc.sbuf_tensor("sIDB", [128, 128], BF16))
        sMRm = ctx.enter_context(nc.sbuf_tensor("sMRm", [1, TSP], F32))
        sMRr = ctx.enter_context(nc.sbuf_tensor("sMRr", [1, TSP], F32))
        sMB = ctx.enter_context(nc.sbuf_tensor("sMB", [128, 2 * TSP], BF16))
        p_mm = [ctx.enter_context(nc.psum_tensor(f"pmm{b}", [128, 512], F32))
                for b in range(NBANK)]
        p_sta = ctx.enter_context(nc.psum_tensor("psta", [1, TSP], F32))
        p_stb = ctx.enter_context(nc.psum_tensor("pstb", [1, TSP], F32))
        p_trf = [ctx.enter_context(nc.psum_tensor(f"ptrf{t}", [128, 1024], BF16))
                 for t in range(2)]
        for c in sem_ws + sem_xt + sem_c + sem_xb + sem_tk + [sem_pe, sem_act, sem_dve, sem_out, sem_warm]:
            c.h = ctx.enter_context(nc.semaphore(c.name))

        ring_next = [0]
        ring_guard = [None] * NSLOT
        bank_guard = [None] * NBANK
        gidx = [0]

        def ring_alloc(n):
            base = ring_next[0]
            skipped = []
            if base + n > NSLOT:
                skipped = list(range(base, NSLOT))
                base = 0
            guards = [ring_guard[s] for s in range(base, base + n)] + \
                     [ring_guard[s] for s in skipped]
            ring_next[0] = (base + n) % NSLOT
            return base, guards

        def ring_set_guard(base, n, guard):
            for s in range(base, base + n):
                ring_guard[s] = guard

        def emit_wdma(phase, c0, ncols, base, guards, q="sync"):
            wait_guards(q, guards)
            dram = wdram[phase]
            ws = sem_ws[base]
            op(q, lambda e, d=dram, c0=c0, n=ncols, b=base, ws=ws:
               e.dma_start(sWr[:, b * 2048:b * 2048 + n], d[:, c0:c0 + n])
               .then_inc(ws.h, 16))
            return (ws, ws.inc(16))

        op("gpsimd", lambda e:
           e.dma_start(sMB[:, 0:128], XBF[0:128, :]).then_inc(sem_warm.h, 16))
        sem_warm.inc(16)

        for j in range(NJ):
            op("sync", lambda e, j=j:
               e.dma_start(sRA[:, (NJ + j) * HID:(NJ + j + 1) * HID],
                           XTOK[j * 128:(j + 1) * 128, :])
               .then_inc(sem_tk[j].h, 16))
            sem_tk[j].inc(16)

        op("sync", lambda e: e.dma_start(sCN[:], CONST[:]).then_inc(sem_c[0].h, 16))
        op("sync", lambda e: e.dma_start(sONF[:], ONF[:]).then_inc(sem_c[1].h, 16))
        op("sync", lambda e: e.dma_start(sIDF[:], IDF[:]).then_inc(sem_c[2].h, 16))
        op("sync", lambda e: e.dma_start(sIDB[:], IDB[:]).then_inc(sem_c[3].h, 16))
        for t in range(NJ):
            wait("sync", sem_tk[t], 16)
        for c in sem_c:
            c.inc(16)

        def emit_xbf_loads(sp, q, guards=(), parts=4):
            xb = sem_xb[sp]
            wait_guards(q, guards)
            kc = KH // parts
            for p in range(parts):
                r0 = sp * KH * TSP + p * kc * TSP
                op(q, lambda e, r0=r0, p=p, kc=kc:
                   e.dma_start(sXN[:, p * kc * TSP:(p + 1) * kc * TSP],
                               XBF[r0:r0 + kc * TSP, :], transpose=True)
                   .then_inc(xb.h, 16))
                xb.inc(16)

        def emit_b3_accum(t, gates):
            wait_guards("gpsimd", gates)
            op("gpsimd", lambda e, t=t:
               e.dma_start(sX[:, t * HID:(t + 1) * HID], BB3[:],
                           accum_op=ALU.add)
               .then_inc(sem_xt[t].h, 16))
            sem_xt[t].inc(16)

        def emit_x_load(t, q, guards=()):
            wait_guards(q, guards)
            op(q, lambda e, t=t:
               e.dma_start(sX[:, t * HID:(t + 1) * HID], X[t * 128:t * 128 + 128, :])
               .then_inc(sem_xt[t].h, 16))
            sem_xt[t].inc(16)



        for eng in ("tensor", "scalar", "vector"):
            for c in sem_c:
                wait(eng, c, 16)

        def plan_ln_stats_j(sp, j, v_aggr):
            for _ in (0,):
                t = sp * NJ + j
                sb = t * 32
                if sp == 0:
                    wait("vector", sem_tk[j], 16)
                    src, c0 = sRA, (NJ + j) * HID
                else:
                    wait("vector", sem_xt[t], 16)
                    src, c0 = sX, t * HID
                for q in range(4):
                    op("vector", lambda e, src=src, c0=c0, sb=sb, q=q:
                       e.bn_stats(sST[:, sb + q * 6:sb + q * 6 + 6],
                                  src[:, c0 + q * 512:c0 + (q + 1) * 512])
                       .then_inc(sem_dve.h, 1))
                    sem_dve.inc()
                wait("vector", sem_dve, sem_dve.v)
                op("vector", lambda e, sb=sb:
                   e.bn_aggr(sST[:, sb + 24:sb + 26], sST[:, sb:sb + 24])
                   .then_inc(sem_dve.h, 1))
                v_aggr.append(sem_dve.inc())

        def plan_ln_stats(sp):
            v_aggr = []
            for j in range(NJ):
                plan_ln_stats_j(sp, j, v_aggr)
            return v_aggr

        def plan_rstd_j(sp, j, v_aggr_j):
            sb = (sp * NJ + j) * 32
            wait("scalar", sem_dve, v_aggr_j)
            op("scalar", lambda e, sb=sb:
               e.activation(sST[:, sb + 26:sb + 27], sST[:, sb + 25:sb + 26],
                            AF.Ln, bias=sCN[:, 96:97], scale=1.0)
               .then_inc(sem_act.h, 1))
            wait("scalar", sem_act, sem_act.inc())
            op("scalar", lambda e, sb=sb:
               e.activation(sST[:, sb + 27:sb + 28], sST[:, sb + 26:sb + 27],
                            AF.Exp, bias=0.0, scale=-0.5)
               .then_inc(sem_act.h, 1))
            return sem_act.inc()

        def plan_ln_rstd_all(sp, v_aggr):
            v_ln = {}
            out = {}

            def emit_ln(j):
                sb = (sp * NJ + j) * 32
                wait("scalar", sem_dve, v_aggr[j])
                op("scalar", lambda e, sb=sb:
                   e.activation(sST[:, sb + 26:sb + 27], sST[:, sb + 25:sb + 26],
                                AF.Ln, bias=sCN[:, 96:97], scale=1.0)
                   .then_inc(sem_act.h, 1))
                v_ln[j] = sem_act.inc()

            def emit_exp(j):
                sb = (sp * NJ + j) * 32
                wait("scalar", sem_act, v_ln[j])
                op("scalar", lambda e, sb=sb:
                   e.activation(sST[:, sb + 27:sb + 28], sST[:, sb + 26:sb + 27],
                                AF.Exp, bias=0.0, scale=-0.5)
                   .then_inc(sem_act.h, 1))
                out[j] = sem_act.inc()

            emit_ln(0)
            for j in range(1, NJ):
                emit_ln(j)
                emit_exp(j - 1)
            emit_exp(NJ - 1)
            return [out[j] for j in range(NJ)]

        bc_state = {}

        def plan_ln_bcast_a(sp, v_rstd, guards_pe=(), guards_dve=()):
            wait_guards("tensor", list(guards_pe))
            for j in range(NJ):
                t = sp * NJ + j
                sb = t * 32
                wait("tensor", sem_act, v_rstd[j])
                op("tensor", lambda e, sb=sb, j=j:
                   e.transpose(p_sta[0:1, j * 128:(j + 1) * 128],
                               sST[:, sb + 24:sb + 25], sIDF[:]))
                op("tensor", lambda e, sb=sb, j=j:
                   e.transpose(p_stb[0:1, j * 128:(j + 1) * 128],
                               sST[:, sb + 27:sb + 28], sIDF[:])
                   .then_inc(sem_pe.h, 1))
                sem_pe.inc()
            v_tr = sem_pe.v
            wait("vector", sem_pe, v_tr)
            wait_guards("vector", list(guards_dve))
            op("vector", lambda e: e.tensor_copy(sMRm[:], p_sta[0:1, :])
               .then_inc(sem_dve.h, 1))
            sem_dve.inc()
            op("vector", lambda e: e.tensor_copy(sMRr[:], p_stb[0:1, :])
               .then_inc(sem_dve.h, 1))
            v_rows = sem_dve.inc()
            wait("vector", sem_dve, v_rows)
            op("vector", lambda e: e.tensor_mul(sMRm[:], sMRm[:], sMRr[:])
               .then_inc(sem_dve.h, 1))
            v_mr = sem_dve.inc()
            bc_state[sp] = (v_mr, v_rows)

        def plan_ln_bcast_b(sp):
            v_mr, v_rows = bc_state[sp]
            for src, dstcol, gate in ((sMRm, 0, v_mr), (sMRr, TSP, v_rows)):
                b = gidx[0] % NBANK
                wait_guards("tensor", [bank_guard[b]])
                wait("tensor", sem_dve, gate)
                op("tensor", lambda e, b=b, src=src:
                   e.matmul(p_mm[b][:], sONF[:], src[:], start=True, stop=True)
                   .then_inc(sem_pe.h, 1))
                v_g = sem_pe.inc()
                gidx[0] += 1
                wait("vector", sem_pe, v_g)
                op("vector", lambda e, b=b, dstcol=dstcol:
                   e.tensor_copy(sMB[:, dstcol:dstcol + TSP], p_mm[b][:])
                   .then_inc(sem_dve.h, 1))
                bank_guard[b] = (sem_dve, sem_dve.inc())
            return sem_dve.v

        def plan_ln_norm(sp, v_bc, chunk_gates=None):
            if chunk_gates is None:
                wait("vector", sem_xb[sp], sem_xb[sp].v)
            wait("vector", sem_dve, v_bc)
            v_mul = []
            for k in range(KH):
                if chunk_gates is not None:
                    wait_guards("vector", [chunk_gates[k]])
                op("vector", lambda e, k=k:
                   e.tensor_tensor(sXN[:, k * TSP:(k + 1) * TSP],
                                   sXN[:, k * TSP:(k + 1) * TSP],
                                   sMB[:, TSP:2 * TSP], ALU.mult)
                   .then_inc(sem_dve.h, 1))
                v_mul.append(sem_dve.inc())
            for k in range(KH):
                wait("vector", sem_dve, v_mul[k])
                op("vector", lambda e, k=k:
                   e.tensor_tensor(sXN[:, k * TSP:(k + 1) * TSP],
                                   sXN[:, k * TSP:(k + 1) * TSP],
                                   sMB[:, 0:TSP], ALU.subtract)
                   .then_inc(sem_dve.h, 1))
                sem_dve.inc()
            return sem_dve.v

        def plan_wmatmul(phase, n_g, n_k, rhs_tensor, evac, first_gates,
                         staged_gate=None, hooks=None, early_q=0):
            hooks = hooks or {}
            nslots = (n_k * 128) // 2048
            for g in range(n_g):
                base, guards = ring_alloc(nslots)
                wv = emit_wdma(phase, g * n_k * 128, n_k * 128, base, guards,
                               q=("gpsimd" if g < early_q else "sync"))
                b = gidx[0] % NBANK
                if g == 0:
                    wait_guards("tensor", first_gates)
                wait_guards("tensor", [bank_guard[b]])
                wait_guards("tensor", [wv])
                for k in range(n_k):
                    if g == 0 and staged_gate is not None and k % 8 == 0:
                        if callable(staged_gate):
                            gd = staged_gate(k)
                            if gd is not None:
                                wait("tensor", gd[0], gd[1])
                        else:
                            ctr, gbase, tot = staged_gate
                            wait("tensor", ctr, gbase + min(k + 8, tot))
                    def mm(e, b=b, base=base, k=k, last=(k == n_k - 1), n_k=n_k):
                        ins = e.matmul(
                            p_mm[b][:],
                            sWr[:, base * 2048 + k * 128:base * 2048 + (k + 1) * 128],
                            rhs_tensor[:, k * TSP:(k + 1) * TSP],
                            start=(k == 0), stop=last)
                        if last:
                            ins.then_inc(sem_pe.h, 1)
                    op("tensor", mm)
                v_grp = sem_pe.inc()
                gidx[0] += 1
                ring_set_guard(base, nslots, (sem_pe, v_grp))
                bank_guard[b] = evac(g, b, v_grp)
                if g + 1 in hooks:
                    hooks[g + 1]()
            return sem_pe.v

        amm_staged_gate = [None]

        def plan_amatmul(phase, with_b3, evac, first_gates, hooks=None,
                         staged_gate=None):
            hooks = hooks or {}
            amm_staged_gate[0] = staged_gate
            ngroup = 0
            for gp in range(4):
                qinfo = []
                for q in range(4):
                    base, guards = ring_alloc(2)
                    wv = emit_wdma(phase, gp * 16384 + q * 4096, 4096, base, guards)
                    qinfo.append((base, wv))
                for j in range(NJ):
                    b = gidx[0] % NBANK
                    if gp == 0 and j == 0:
                        wait_guards("tensor", first_gates)
                        sg = amm_staged_gate[0]
                    else:
                        sg = None
                    wait_guards("tensor", [bank_guard[b]])
                    last_j = (j == NJ - 1)
                    for k in range(KI):
                        if k % 8 == 0 and j == 0:
                            wait_guards("tensor", [qinfo[k // 8][1]])
                        if sg is not None and k % 16 == 0:
                            ctr, gbase, tot = sg
                            wait("tensor", ctr, gbase + min(k + 16, tot))
                        base = qinfo[k // 8][0]
                        kk = k % 8
                        inc = (k == KI - 1) or (last_j and kk == 7)

                        def mm(e, b=b, base=base, kk=kk, k=k, j=j,
                               st=(k == 0), last=(k == KI - 1),
                               inc=inc):
                            ins = e.matmul(
                                p_mm[b][:],
                                stat_sRB[:, k * TSP + j * 128:k * TSP + (j + 1) * 128],
                                sWr[:, base * 2048 + kk * 512:base * 2048 + (kk + 1) * 512],
                                start=st, stop=last)
                            if inc:
                                ins.then_inc(sem_pe.h, 1)
                        op("tensor", mm)
                        if inc:
                            v_now = sem_pe.inc()
                            if last_j:
                                ring_set_guard(qinfo[k // 8][0], 2, (sem_pe, v_now))
                    v_grp = sem_pe.v
                    gidx[0] += 1
                    bank_guard[b] = evac(gp, j, b, v_grp)
                    ngroup += 1
                    if ngroup in hooks:
                        hooks[ngroup]()

        stat_sRB = sRB

        def evac_m1(g, b, v_grp):
            wait("scalar", sem_pe, v_grp)
            if g < KI:
                op("scalar", lambda e, g=g, b=b:
                   e.activation(sRA[:, g * TSP:(g + 1) * TSP], p_mm[b][:],
                                AF.Copy, bias=0.0, scale=1.0)
                   .then_inc(sem_act.h, 1))
            else:
                op("scalar", lambda e, g=g, b=b:
                   e.activation(sRB[:, (g - KI) * TSP:(g - KI + 1) * TSP], p_mm[b][:],
                                AF.Silu, bias=0.0, scale=1.0)
                   .then_inc(sem_act.h, 1))
            return (sem_act, sem_act.inc())

        def mk_evac_m2(m1_evac_base):
            def evac_m2(g, b, v_grp):
                wait("vector", sem_pe, v_grp)
                wait("vector", sem_act, m1_evac_base + KI + g + 1)
                op("vector", lambda e, g=g, b=b:
                   e.scalar_tensor_tensor(sRB[:, g * TSP:(g + 1) * TSP], p_mm[b][:],
                                          sCN[:, 64 + g:65 + g],
                                          sRB[:, g * TSP:(g + 1) * TSP],
                                          ALU.mult, ALU.mult)
                   .then_inc(sem_dve.h, 1))
                return (sem_dve, sem_dve.inc())
            return evac_m2

        def mk_evac_relu(bcol, dst):
            def evac_relu(g, b, v_grp):
                wait("scalar", sem_pe, v_grp)
                op("scalar", lambda e, g=g, b=b, bcol=bcol, dst=dst:
                   e.activation(dst[:, g * TSP:(g + 1) * TSP], p_mm[b][:],
                                AF.Relu, bias=sCN[:, bcol + g:bcol + g + 1], scale=1.0)
                   .then_inc(sem_act.h, 1))
                return (sem_act, sem_act.inc())
            return evac_relu

        def mk_evac_resid(sp, store):
            def evac_resid(gp, j, b, v_grp):
                t = sp * NJ + j
                col = t * HID + gp * 512
                if gp == 0:
                    wait("vector", sem_xt[t], 32)
                wait("vector", sem_pe, v_grp)
                op("vector", lambda e, col=col, b=b:
                   e.tensor_tensor(sX[:, col:col + 512], p_mm[b][:],
                                   sX[:, col:col + 512], ALU.add)
                   .then_inc(sem_dve.h, 1))
                v = sem_dve.inc()
                if store:
                    oq = "sync" if (sp == 1 and gp == 3 and j % 2 == 0) \
                        else "gpsimd"
                    wait(oq, sem_dve, v)
                    op(oq, lambda e, t=t, gp=gp, col=col:
                       e.dma_start(OUT[t * 128:t * 128 + 128, gp * 512:(gp + 1) * 512],
                                   sX[:, col:col + 512])
                       .then_inc(sem_out.h, 16))
                    sem_out.inc(16)
                return (sem_dve, v)
            return evac_resid

        trf_guard = [None, None]
        trf_quad_ev = {}

        def emit_xn_tm(j, v_rstd):
            sb = j * 32
            wait("vector", sem_act, v_rstd)
            op("vector", lambda e, j=j, sb=sb:
               e.tensor_scalar(sRA[:, j * HID:(j + 1) * HID],
                               sRA[:, (NJ + j) * HID:(NJ + j + 1) * HID],
                               sST[:, sb + 24:sb + 25],
                               sST[:, sb + 27:sb + 28],
                               ALU.subtract, ALU.mult)
               .then_inc(sem_dve.h, 1))
            return sem_dve.inc()

        def emit_trf(j, v_xntm):
            wait("tensor", sem_dve, v_xntm)
            for q in range(KH // 8):
                ts = (j * (KH // 8) + q) % 2
                wait_guards("tensor", [trf_guard[ts]])
                for kk in range(8):
                    k = q * 8 + kk
                    last = kk == 7

                    def trf(e, j=j, k=k, ts=ts, kk=kk, last=last):
                        ins = e.transpose(
                            p_trf[ts][:, kk * 128:(kk + 1) * 128],
                            sRA[:, j * HID + k * 128:j * HID + (k + 1) * 128],
                            sIDB[:])
                        if last:
                            ins.then_inc(sem_pe.h, 1)
                    op("tensor", trf)
                sem_pe.inc()
                v_tr = sem_pe.v
                wait("scalar", sem_pe, v_tr)
                k0 = q * 8
                dst = bass.AP(sXN, k0 * TSP + j * 128,
                              [[sXN.shape[1], 128], [TSP, 8], [1, 128]])
                op("scalar", lambda e, dst=dst, ts=ts:
                   e.activation(dst, p_trf[ts][:, 0:1024],
                                AF.Copy, bias=0.0, scale=1.0)
                   .then_inc(sem_act.h, 1))
                trf_guard[ts] = (sem_act, sem_act.inc())
                if j == NJ - 1:
                    trf_quad_ev[q] = sem_act.v

        v_aggr0 = []
        rstd0 = {}
        plan_ln_stats_j(0, 0, v_aggr0)
        plan_ln_stats_j(0, 1, v_aggr0)
        rstd0[0] = plan_rstd_j(0, 0, v_aggr0[0])
        vx = {0: emit_xn_tm(0, rstd0[0])}
        plan_ln_stats_j(0, 2, v_aggr0)
        rstd0[1] = plan_rstd_j(0, 1, v_aggr0[1])
        vx[1] = emit_xn_tm(1, rstd0[1])
        plan_ln_stats_j(0, 3, v_aggr0)
        rstd0[2] = plan_rstd_j(0, 2, v_aggr0[2])
        vx[2] = emit_xn_tm(2, rstd0[2])
        rstd0[3] = plan_rstd_j(0, 3, v_aggr0[3])
        vx[3] = emit_xn_tm(3, rstd0[3])
        for j in range(NJ):
            emit_trf(j, vx[j])
        nrm = {0: (sem_act, sem_act.v)}
        v_aggr1 = None
        for sp in range(NSP):
            m1_evac_base = sem_act.v
            if sp == 0:
                def m1_fn(k):
                    return (sem_act, trf_quad_ev[min((k + 7) // 8, KH // 8 - 1)])
                plan_wmatmul(1, 64, KH, sXN, evac_m1, [], staged_gate=m1_fn,
                             early_q=6)
            else:
                plan_wmatmul(1, 64, KH, sXN, evac_m1, [nrm[sp]])
            m1_xa_done = m1_evac_base + KI
            if sp == 0:
                m1_end0 = sem_pe.v
                for t in range(2 * NJ):
                    off = 1 + 2 * t if t < NJ else 9 + 4 * (t - NJ)
                    emit_x_load(t, "gpsimd", guards=[(sem_pe, m1_end0 + off)])
                for j in range(NJ):
                    emit_b3_accum(j, [(sem_xt[j], 16)])
            m2_dve_base = sem_dve.v
            plan_wmatmul(2, KI, KI, sRA, mk_evac_m2(m1_evac_base), [],
                         staged_gate=(sem_act, m1_evac_base, KI))
            if sp == 0:
                v_aggr1 = plan_ln_stats(1)
                for j in range(NJ):
                    emit_b3_accum(NJ + j, [(sem_xt[NJ + j], 16),
                                           (sem_dve, v_aggr1[j])])
                rstd1 = plan_ln_rstd_all(1, v_aggr1)
            plan_amatmul(3, False, mk_evac_resid(sp, False), [],
                         staged_gate=(sem_dve, m2_dve_base, KI))
            m4_act_base = sem_act.v
            m4_end = plan_wmatmul(4, KI, KH, sXN, mk_evac_relu(0, sRA), [])
            m5_act_base = None
            m5_in_gate = (sem_act, m4_act_base, KI)
            m5_hooks = {}
            if sp == 0:
                bc1 = [None]
                m5_hooks[4] = lambda: plan_ln_bcast_a(1, rstd1)
                def _bc1b():
                    bc1[0] = plan_ln_bcast_b(1)
                m5_hooks[8] = _bc1b
            m5_act_base = sem_act.v
            plan_wmatmul(5, KI, KI, sRA, mk_evac_relu(32, sRB), [],
                         staged_gate=m5_in_gate, hooks=m5_hooks)
            if sp == 0:
                emit_xbf_loads(1, "sync", guards=[(sem_pe, m4_end)])
            if sp == 0:
                def _hook():
                    nrm[1] = (sem_dve, plan_ln_norm(1, bc1[0]))
                plan_amatmul(6, False, mk_evac_resid(sp, True), [],
                             staged_gate=(sem_act, m5_act_base, KI),
                             hooks={8: _hook})
            else:
                plan_amatmul(6, False, mk_evac_resid(sp, True), [],
                             staged_gate=(sem_act, m5_act_base, KI))

        wait("gpsimd", sem_out, sem_out.v)

        block = ctx.enter_context(nc.Block())

        @block.sync
        def _(e):
            for fn in plan["sync"]:
                fn(e)

        @block.gpsimd
        def _(e):
            for fn in plan["gpsimd"]:
                fn(e)

        @block.tensor
        def _(e):
            for fn in plan["tensor"]:
                fn(e)

        @block.vector
        def _(e):
            for fn in plan["vector"]:
                fn(e)

        @block.scalar
        def _(e):
            for fn in plan["scalar"]:
                fn(e)

    return nc


def _stage_stationary(W, n_g, n_k):
    return np.ascontiguousarray(
        W.reshape(n_g, 128, n_k, 128).transpose(3, 0, 2, 1).reshape(128, n_g * n_k * 128)
    ).astype(BF)


def _stage_moving(W):
    return np.ascontiguousarray(
        W.reshape(4, 512, 32, 128).transpose(3, 0, 2, 1).reshape(128, 4 * 32 * 512)
    ).astype(BF)


def kernel(**inputs):
    X = np.asarray(inputs["L0_attn_hidden_states"], np.float32).reshape(SEQ, HID)
    W_in = np.asarray(inputs["L0_attn_W_in"], np.float32)
    conv_w = np.asarray(inputs["L0_attn_conv_w"], np.float32)
    W_out = np.asarray(inputs["L0_attn_W_out"], np.float32)
    D = np.asarray(inputs["L0_attn_D"], np.float32)
    W1 = np.asarray(inputs["L0_ffn_W1"], np.float32)
    b1 = np.asarray(inputs["L0_ffn_b1"], np.float32)
    W2 = np.asarray(inputs["L0_ffn_W2"], np.float32)
    b2 = np.asarray(inputs["L0_ffn_b2"], np.float32)
    W3 = np.asarray(inputs["L0_ffn_W3"], np.float32)
    b3 = np.asarray(inputs["L0_ffn_b3"], np.float32)
    _unused = (inputs["L0_attn_W_x"], inputs["L0_attn_W_dt"],
               inputs["L0_attn_b_dt"])
    del _unused

    W1S = _stage_stationary(W_in, 64, 16)
    W2S = _stage_stationary(conv_w, 32, 32)
    W3S = _stage_moving(W_out)
    W4S = _stage_stationary(W1, 32, 16)
    W5S = _stage_stationary(W2, 32, 32)
    W6S = _stage_moving(W3)
    consts = np.zeros((128, 97), np.float32)
    consts[:, 96] = 1e-5
    consts[:, 0:32] = b1.reshape(32, 128).T
    consts[:, 32:64] = b2.reshape(32, 128).T
    consts[:, 64:96] = D.reshape(32, 128).T
    identf = np.eye(128, dtype=np.float32)
    identb = identf.astype(BF)
    onesf = np.ones((1, 128), np.float32)
    bb3 = np.ascontiguousarray(np.broadcast_to(b3, (128, HID)))

    nc = build_program()
    in_maps = []
    for c in range(N_CORES):
        xc = np.ascontiguousarray(X[c * T_CORE:(c + 1) * T_CORE])
        xtok = xc.astype(BF)
        xbf = np.ascontiguousarray(
            xc.astype(BF).reshape(NSP, TSP, KH, 128).transpose(0, 2, 1, 3)
        ).reshape(KH * T_CORE, 128)
        in_maps.append({
            "X": xc, "XBF": xbf, "XTOK": xtok[:TSP],
            "W1S": W1S, "W2S": W2S, "W3S": W3S,
            "W4S": W4S, "W5S": W5S, "W6S": W6S,
            "CONST": consts, "IDF": identf, "IDB": identb, "ONF": onesf, "BB3": bb3,
        })
    res = run_bass_kernel_spmd(nc, in_maps, core_ids=list(range(N_CORES)))
    out = np.concatenate([np.asarray(r["OUT"]) for r in res.results], axis=0)
    return out.reshape(1, SEQ, HID).astype(np.float32)


# version 42
# speedup vs baseline: 1.0062x; 1.0062x over previous
import numpy as np
import ml_dtypes
from contextlib import ExitStack
import concourse.bass as bass
import concourse.mybir as mybir
from concourse.bass_utils import run_bass_kernel_spmd

F32 = mybir.dt.float32
BF16 = mybir.dt.bfloat16
AF = mybir.ActivationFunctionType
ALU = mybir.AluOpType
BF = ml_dtypes.bfloat16

HID = 2048
INTER = 4096
SEQ = 8192
N_CORES = 8
T_CORE = SEQ // N_CORES
NSP = 2
TSP = 512
NJ = TSP // 128
KH = HID // 128
KI = INTER // 128
NSLOT = 12
NBANK = 4


class Ctr:
    def __init__(self, name):
        self.name = name
        self.v = 0
        self.h = None

    def inc(self, n=1):
        self.v += n
        return self.v


def build_program():
    nc = bass.Bass("TRN2", target_bir_lowering=False)
    X = nc.dram_tensor("X", [T_CORE, HID], F32, kind="ExternalInput")
    W1S = nc.dram_tensor("W1S", [128, 64 * 16 * 128], BF16, kind="ExternalInput")
    W2S = nc.dram_tensor("W2S", [128, 32 * 32 * 128], BF16, kind="ExternalInput")
    W3S = nc.dram_tensor("W3S", [128, 4 * 32 * 512], BF16, kind="ExternalInput")
    W4S = nc.dram_tensor("W4S", [128, 32 * 16 * 128], BF16, kind="ExternalInput")
    W5S = nc.dram_tensor("W5S", [128, 32 * 32 * 128], BF16, kind="ExternalInput")
    W6S = nc.dram_tensor("W6S", [128, 4 * 32 * 512], BF16, kind="ExternalInput")
    CONST = nc.dram_tensor("CONST", [128, 97], F32, kind="ExternalInput")
    XBF = nc.dram_tensor("XBF", [KH * T_CORE, 128], BF16, kind="ExternalInput")
    XTOK = nc.dram_tensor("XTOK", [TSP, HID], BF16, kind="ExternalInput")
    ONF = nc.dram_tensor("ONF", [1, 128], F32, kind="ExternalInput")
    IDF = nc.dram_tensor("IDF", [128, 128], F32, kind="ExternalInput")
    IDB = nc.dram_tensor("IDB", [128, 128], BF16, kind="ExternalInput")
    BB3 = nc.dram_tensor("BB3", [128, HID], F32, kind="ExternalInput")
    OUT = nc.dram_tensor("OUT", [T_CORE, HID], F32, kind="ExternalOutput")

    wdram = {1: W1S, 2: W2S, 3: W3S, 4: W4S, 5: W5S, 6: W6S}

    plan = {k: [] for k in ("sync", "gpsimd", "tensor", "vector", "scalar")}

    def op(eng, fn):
        plan[eng].append(fn)

    def wait(eng, ctr, val):
        if val > 0:
            op(eng, lambda e, s=ctr, v=val: e.wait_ge(s.h, v))

    def wait_guards(eng, guards):
        agg = {}
        for g in guards:
            if g is None:
                continue
            s, v = g
            agg[s] = max(agg.get(s, 0), v)
        for s, v in agg.items():
            wait(eng, s, v)

    sem_ws = [Ctr(f"ws{i}") for i in range(NSLOT)]
    sem_xt = [Ctr(f"xt{i}") for i in range(NSP * NJ)]
    sem_c = [Ctr(f"c{i}") for i in range(4)]
    sem_xb = [Ctr(f"xb{i}") for i in range(NSP)]
    sem_tk = [Ctr(f"tk{i}") for i in range(NJ)]
    sem_pe = Ctr("pe")
    sem_act = Ctr("act")
    sem_dve = Ctr("dve")
    sem_out = Ctr("out")
    sem_warm = Ctr("warm")

    with ExitStack() as ctx:
        sX = ctx.enter_context(nc.sbuf_tensor("sX", [128, NSP * NJ * HID], F32))
        sXN = ctx.enter_context(nc.sbuf_tensor("sXN", [128, KH * TSP], BF16))
        sRA = ctx.enter_context(nc.sbuf_tensor("sRA", [128, KI * TSP], BF16))
        sRB = ctx.enter_context(nc.sbuf_tensor("sRB", [128, KI * TSP], BF16))
        sWr = ctx.enter_context(nc.sbuf_tensor("sWr", [128, NSLOT * 2048], BF16))
        sST = ctx.enter_context(nc.sbuf_tensor("sST", [128, NSP * NJ * 32], F32))
        sCN = ctx.enter_context(nc.sbuf_tensor("sCN", [128, 97], F32))
        sONF = ctx.enter_context(nc.sbuf_tensor("sONF", [1, 128], F32))
        sIDF = ctx.enter_context(nc.sbuf_tensor("sIDF", [128, 128], F32))
        sIDB = ctx.enter_context(nc.sbuf_tensor("sIDB", [128, 128], BF16))
        sMRm = ctx.enter_context(nc.sbuf_tensor("sMRm", [1, TSP], F32))
        sMRr = ctx.enter_context(nc.sbuf_tensor("sMRr", [1, TSP], F32))
        sMB = ctx.enter_context(nc.sbuf_tensor("sMB", [128, 2 * TSP], BF16))
        p_mm = [ctx.enter_context(nc.psum_tensor(f"pmm{b}", [128, 512], F32))
                for b in range(NBANK)]
        p_sta = ctx.enter_context(nc.psum_tensor("psta", [1, TSP], F32))
        p_stb = ctx.enter_context(nc.psum_tensor("pstb", [1, TSP], F32))
        p_trf = [ctx.enter_context(nc.psum_tensor(f"ptrf{t}", [128, 1024], BF16))
                 for t in range(2)]
        for c in sem_ws + sem_xt + sem_c + sem_xb + sem_tk + [sem_pe, sem_act, sem_dve, sem_out, sem_warm]:
            c.h = ctx.enter_context(nc.semaphore(c.name))

        ring_next = [0]
        ring_guard = [None] * NSLOT
        bank_guard = [None] * NBANK
        gidx = [0]

        def ring_alloc(n):
            base = ring_next[0]
            skipped = []
            if base + n > NSLOT:
                skipped = list(range(base, NSLOT))
                base = 0
            guards = [ring_guard[s] for s in range(base, base + n)] + \
                     [ring_guard[s] for s in skipped]
            ring_next[0] = (base + n) % NSLOT
            return base, guards

        def ring_set_guard(base, n, guard):
            for s in range(base, base + n):
                ring_guard[s] = guard

        def emit_wdma(phase, c0, ncols, base, guards, q="sync"):
            wait_guards(q, guards)
            dram = wdram[phase]
            ws = sem_ws[base]
            op(q, lambda e, d=dram, c0=c0, n=ncols, b=base, ws=ws:
               e.dma_start(sWr[:, b * 2048:b * 2048 + n], d[:, c0:c0 + n])
               .then_inc(ws.h, 16))
            return (ws, ws.inc(16))

        op("gpsimd", lambda e:
           e.dma_start(sMB[:, 0:128], XBF[0:128, :]).then_inc(sem_warm.h, 16))
        sem_warm.inc(16)

        for j in range(NJ):
            op("sync", lambda e, j=j:
               e.dma_start(sRA[:, (NJ + j) * HID:(NJ + j + 1) * HID],
                           XTOK[j * 128:(j + 1) * 128, :])
               .then_inc(sem_tk[j].h, 16))
            sem_tk[j].inc(16)

        op("sync", lambda e: e.dma_start(sCN[:], CONST[:]).then_inc(sem_c[0].h, 16))
        op("sync", lambda e: e.dma_start(sONF[:], ONF[:]).then_inc(sem_c[1].h, 16))
        op("sync", lambda e: e.dma_start(sIDF[:], IDF[:]).then_inc(sem_c[2].h, 16))
        op("sync", lambda e: e.dma_start(sIDB[:], IDB[:]).then_inc(sem_c[3].h, 16))
        for t in range(NJ):
            wait("sync", sem_tk[t], 16)
        for c in sem_c:
            c.inc(16)

        def emit_xbf_loads(sp, q, guards=(), parts=4):
            xb = sem_xb[sp]
            wait_guards(q, guards)
            kc = KH // parts
            for p in range(parts):
                r0 = sp * KH * TSP + p * kc * TSP
                op(q, lambda e, r0=r0, p=p, kc=kc:
                   e.dma_start(sXN[:, p * kc * TSP:(p + 1) * kc * TSP],
                               XBF[r0:r0 + kc * TSP, :], transpose=True)
                   .then_inc(xb.h, 16))
                xb.inc(16)

        def emit_b3_accum(t, gates):
            wait_guards("gpsimd", gates)
            op("gpsimd", lambda e, t=t:
               e.dma_start(sX[:, t * HID:(t + 1) * HID], BB3[:],
                           accum_op=ALU.add)
               .then_inc(sem_xt[t].h, 16))
            sem_xt[t].inc(16)

        def emit_x_load(t, q, guards=()):
            wait_guards(q, guards)
            op(q, lambda e, t=t:
               e.dma_start(sX[:, t * HID:(t + 1) * HID], X[t * 128:t * 128 + 128, :])
               .then_inc(sem_xt[t].h, 16))
            sem_xt[t].inc(16)



        for eng in ("tensor", "scalar", "vector"):
            for c in sem_c:
                wait(eng, c, 16)

        def plan_ln_stats_j(sp, j, v_aggr):
            for _ in (0,):
                t = sp * NJ + j
                sb = t * 32
                if sp == 0:
                    wait("vector", sem_tk[j], 16)
                    src, c0 = sRA, (NJ + j) * HID
                else:
                    wait("vector", sem_xt[t], 16)
                    src, c0 = sX, t * HID
                for q in range(4):
                    op("vector", lambda e, src=src, c0=c0, sb=sb, q=q:
                       e.bn_stats(sST[:, sb + q * 6:sb + q * 6 + 6],
                                  src[:, c0 + q * 512:c0 + (q + 1) * 512])
                       .then_inc(sem_dve.h, 1))
                    sem_dve.inc()
                wait("vector", sem_dve, sem_dve.v)
                op("vector", lambda e, sb=sb:
                   e.bn_aggr(sST[:, sb + 24:sb + 26], sST[:, sb:sb + 24])
                   .then_inc(sem_dve.h, 1))
                v_aggr.append(sem_dve.inc())

        def plan_ln_stats(sp):
            v_aggr = []
            for j in range(NJ):
                plan_ln_stats_j(sp, j, v_aggr)
            return v_aggr

        def plan_rstd_j(sp, j, v_aggr_j):
            sb = (sp * NJ + j) * 32
            wait("scalar", sem_dve, v_aggr_j)
            op("scalar", lambda e, sb=sb:
               e.activation(sST[:, sb + 26:sb + 27], sST[:, sb + 25:sb + 26],
                            AF.Ln, bias=sCN[:, 96:97], scale=1.0)
               .then_inc(sem_act.h, 1))
            wait("scalar", sem_act, sem_act.inc())
            op("scalar", lambda e, sb=sb:
               e.activation(sST[:, sb + 27:sb + 28], sST[:, sb + 26:sb + 27],
                            AF.Exp, bias=0.0, scale=-0.5)
               .then_inc(sem_act.h, 1))
            return sem_act.inc()

        def plan_ln_rstd_all(sp, v_aggr):
            v_ln = {}
            out = {}

            def emit_ln(j):
                sb = (sp * NJ + j) * 32
                wait("scalar", sem_dve, v_aggr[j])
                op("scalar", lambda e, sb=sb:
                   e.activation(sST[:, sb + 26:sb + 27], sST[:, sb + 25:sb + 26],
                                AF.Ln, bias=sCN[:, 96:97], scale=1.0)
                   .then_inc(sem_act.h, 1))
                v_ln[j] = sem_act.inc()

            def emit_exp(j):
                sb = (sp * NJ + j) * 32
                wait("scalar", sem_act, v_ln[j])
                op("scalar", lambda e, sb=sb:
                   e.activation(sST[:, sb + 27:sb + 28], sST[:, sb + 26:sb + 27],
                                AF.Exp, bias=0.0, scale=-0.5)
                   .then_inc(sem_act.h, 1))
                out[j] = sem_act.inc()

            emit_ln(0)
            for j in range(1, NJ):
                emit_ln(j)
                emit_exp(j - 1)
            emit_exp(NJ - 1)
            return [out[j] for j in range(NJ)]

        bc_state = {}

        def plan_ln_bcast_a(sp, v_rstd, guards_pe=(), guards_dve=()):
            wait_guards("tensor", list(guards_pe))
            for j in range(NJ):
                t = sp * NJ + j
                sb = t * 32
                wait("tensor", sem_act, v_rstd[j])
                op("tensor", lambda e, sb=sb, j=j:
                   e.transpose(p_sta[0:1, j * 128:(j + 1) * 128],
                               sST[:, sb + 24:sb + 25], sIDF[:]))
                op("tensor", lambda e, sb=sb, j=j:
                   e.transpose(p_stb[0:1, j * 128:(j + 1) * 128],
                               sST[:, sb + 27:sb + 28], sIDF[:])
                   .then_inc(sem_pe.h, 1))
                sem_pe.inc()
            v_tr = sem_pe.v
            wait("vector", sem_pe, v_tr)
            wait_guards("vector", list(guards_dve))
            op("vector", lambda e: e.tensor_copy(sMRm[:], p_sta[0:1, :])
               .then_inc(sem_dve.h, 1))
            sem_dve.inc()
            op("vector", lambda e: e.tensor_copy(sMRr[:], p_stb[0:1, :])
               .then_inc(sem_dve.h, 1))
            v_rows = sem_dve.inc()
            wait("vector", sem_dve, v_rows)
            op("vector", lambda e: e.tensor_mul(sMRm[:], sMRm[:], sMRr[:])
               .then_inc(sem_dve.h, 1))
            v_mr = sem_dve.inc()
            bc_state[sp] = (v_mr, v_rows)

        def plan_ln_bcast_b(sp):
            v_mr, v_rows = bc_state[sp]
            for src, dstcol, gate in ((sMRm, 0, v_mr), (sMRr, TSP, v_rows)):
                b = gidx[0] % NBANK
                wait_guards("tensor", [bank_guard[b]])
                wait("tensor", sem_dve, gate)
                op("tensor", lambda e, b=b, src=src:
                   e.matmul(p_mm[b][:], sONF[:], src[:], start=True, stop=True)
                   .then_inc(sem_pe.h, 1))
                v_g = sem_pe.inc()
                gidx[0] += 1
                wait("vector", sem_pe, v_g)
                op("vector", lambda e, b=b, dstcol=dstcol:
                   e.tensor_copy(sMB[:, dstcol:dstcol + TSP], p_mm[b][:])
                   .then_inc(sem_dve.h, 1))
                bank_guard[b] = (sem_dve, sem_dve.inc())
            return sem_dve.v

        def plan_ln_norm(sp, v_bc, chunk_gates=None):
            if chunk_gates is None:
                wait("vector", sem_xb[sp], sem_xb[sp].v)
            wait("vector", sem_dve, v_bc)
            v_mul = []
            for k in range(KH):
                if chunk_gates is not None:
                    wait_guards("vector", [chunk_gates[k]])
                op("vector", lambda e, k=k:
                   e.tensor_tensor(sXN[:, k * TSP:(k + 1) * TSP],
                                   sXN[:, k * TSP:(k + 1) * TSP],
                                   sMB[:, TSP:2 * TSP], ALU.mult)
                   .then_inc(sem_dve.h, 1))
                v_mul.append(sem_dve.inc())
            for k in range(KH):
                wait("vector", sem_dve, v_mul[k])
                op("vector", lambda e, k=k:
                   e.tensor_tensor(sXN[:, k * TSP:(k + 1) * TSP],
                                   sXN[:, k * TSP:(k + 1) * TSP],
                                   sMB[:, 0:TSP], ALU.subtract)
                   .then_inc(sem_dve.h, 1))
                sem_dve.inc()
            return sem_dve.v

        def plan_wmatmul(phase, n_g, n_k, rhs_tensor, evac, first_gates,
                         staged_gate=None, hooks=None, early_q=0):
            hooks = hooks or {}
            nslots = (n_k * 128) // 2048
            for g in range(n_g):
                base, guards = ring_alloc(nslots)
                wv = emit_wdma(phase, g * n_k * 128, n_k * 128, base, guards,
                               q=("gpsimd" if g < early_q else "sync"))
                b = gidx[0] % NBANK
                if g == 0:
                    wait_guards("tensor", first_gates)
                wait_guards("tensor", [bank_guard[b]])
                wait_guards("tensor", [wv])
                for k in range(n_k):
                    if g == 0 and staged_gate is not None and k % 8 == 0:
                        if callable(staged_gate):
                            gd = staged_gate(k)
                            if gd is not None:
                                wait("tensor", gd[0], gd[1])
                        else:
                            ctr, gbase, tot = staged_gate
                            wait("tensor", ctr, gbase + min(k + 8, tot))
                    def mm(e, b=b, base=base, k=k, last=(k == n_k - 1), n_k=n_k):
                        ins = e.matmul(
                            p_mm[b][:],
                            sWr[:, base * 2048 + k * 128:base * 2048 + (k + 1) * 128],
                            rhs_tensor[:, k * TSP:(k + 1) * TSP],
                            start=(k == 0), stop=last)
                        if last:
                            ins.then_inc(sem_pe.h, 1)
                    op("tensor", mm)
                v_grp = sem_pe.inc()
                gidx[0] += 1
                ring_set_guard(base, nslots, (sem_pe, v_grp))
                bank_guard[b] = evac(g, b, v_grp)
                if g + 1 in hooks:
                    hooks[g + 1]()
            return sem_pe.v

        amm_staged_gate = [None]

        def plan_amatmul(phase, with_b3, evac, first_gates, hooks=None,
                         staged_gate=None):
            hooks = hooks or {}
            amm_staged_gate[0] = staged_gate
            ngroup = 0
            for gp in range(4):
                qinfo = []
                for q in range(4):
                    base, guards = ring_alloc(2)
                    wv = emit_wdma(phase, gp * 16384 + q * 4096, 4096, base, guards)
                    qinfo.append((base, wv))
                for j in range(NJ):
                    b = gidx[0] % NBANK
                    if gp == 0 and j == 0:
                        wait_guards("tensor", first_gates)
                        sg = amm_staged_gate[0]
                    else:
                        sg = None
                    wait_guards("tensor", [bank_guard[b]])
                    last_j = (j == NJ - 1)
                    for k in range(KI):
                        if k % 8 == 0 and j == 0:
                            wait_guards("tensor", [qinfo[k // 8][1]])
                        if sg is not None and k % 16 == 0:
                            ctr, gbase, tot = sg
                            wait("tensor", ctr, gbase + min(k + 16, tot))
                        base = qinfo[k // 8][0]
                        kk = k % 8
                        inc = (k == KI - 1) or (last_j and kk == 7)

                        def mm(e, b=b, base=base, kk=kk, k=k, j=j,
                               st=(k == 0), last=(k == KI - 1),
                               inc=inc):
                            ins = e.matmul(
                                p_mm[b][:],
                                stat_sRB[:, k * TSP + j * 128:k * TSP + (j + 1) * 128],
                                sWr[:, base * 2048 + kk * 512:base * 2048 + (kk + 1) * 512],
                                start=st, stop=last)
                            if inc:
                                ins.then_inc(sem_pe.h, 1)
                        op("tensor", mm)
                        if inc:
                            v_now = sem_pe.inc()
                            if last_j:
                                ring_set_guard(qinfo[k // 8][0], 2, (sem_pe, v_now))
                    v_grp = sem_pe.v
                    gidx[0] += 1
                    bank_guard[b] = evac(gp, j, b, v_grp)
                    ngroup += 1
                    if ngroup in hooks:
                        hooks[ngroup]()

        stat_sRB = sRB

        def evac_m1(g, b, v_grp):
            wait("scalar", sem_pe, v_grp)
            if g < KI:
                op("scalar", lambda e, g=g, b=b:
                   e.activation(sRA[:, g * TSP:(g + 1) * TSP], p_mm[b][:],
                                AF.Copy, bias=0.0, scale=1.0)
                   .then_inc(sem_act.h, 1))
            else:
                op("scalar", lambda e, g=g, b=b:
                   e.activation(sRB[:, (g - KI) * TSP:(g - KI + 1) * TSP], p_mm[b][:],
                                AF.Silu, bias=0.0, scale=1.0)
                   .then_inc(sem_act.h, 1))
            return (sem_act, sem_act.inc())

        def mk_evac_m2(m1_evac_base):
            def evac_m2(g, b, v_grp):
                wait("vector", sem_pe, v_grp)
                wait("vector", sem_act, m1_evac_base + KI + g + 1)
                op("vector", lambda e, g=g, b=b:
                   e.scalar_tensor_tensor(sRB[:, g * TSP:(g + 1) * TSP], p_mm[b][:],
                                          sCN[:, 64 + g:65 + g],
                                          sRB[:, g * TSP:(g + 1) * TSP],
                                          ALU.mult, ALU.mult)
                   .then_inc(sem_dve.h, 1))
                return (sem_dve, sem_dve.inc())
            return evac_m2

        def mk_evac_relu(bcol, dst):
            def evac_relu(g, b, v_grp):
                wait("scalar", sem_pe, v_grp)
                op("scalar", lambda e, g=g, b=b, bcol=bcol, dst=dst:
                   e.activation(dst[:, g * TSP:(g + 1) * TSP], p_mm[b][:],
                                AF.Relu, bias=sCN[:, bcol + g:bcol + g + 1], scale=1.0)
                   .then_inc(sem_act.h, 1))
                return (sem_act, sem_act.inc())
            return evac_relu

        def mk_evac_resid(sp, store):
            def evac_resid(gp, j, b, v_grp):
                t = sp * NJ + j
                col = t * HID + gp * 512
                if gp == 0:
                    wait("vector", sem_xt[t], 32)
                wait("vector", sem_pe, v_grp)
                op("vector", lambda e, col=col, b=b:
                   e.tensor_tensor(sX[:, col:col + 512], p_mm[b][:],
                                   sX[:, col:col + 512], ALU.add)
                   .then_inc(sem_dve.h, 1))
                v = sem_dve.inc()
                if store:
                    oq = "sync" if (sp == 1 and gp == 3) else "gpsimd"
                    wait(oq, sem_dve, v)
                    op(oq, lambda e, t=t, gp=gp, col=col:
                       e.dma_start(OUT[t * 128:t * 128 + 128, gp * 512:(gp + 1) * 512],
                                   sX[:, col:col + 512])
                       .then_inc(sem_out.h, 16))
                    sem_out.inc(16)
                return (sem_dve, v)
            return evac_resid

        trf_guard = [None, None]
        trf_quad_ev = {}

        def emit_xn_tm(j, v_rstd):
            sb = j * 32
            wait("vector", sem_act, v_rstd)
            op("vector", lambda e, j=j, sb=sb:
               e.tensor_scalar(sRA[:, j * HID:(j + 1) * HID],
                               sRA[:, (NJ + j) * HID:(NJ + j + 1) * HID],
                               sST[:, sb + 24:sb + 25],
                               sST[:, sb + 27:sb + 28],
                               ALU.subtract, ALU.mult)
               .then_inc(sem_dve.h, 1))
            return sem_dve.inc()

        def emit_trf(j, v_xntm):
            wait("tensor", sem_dve, v_xntm)
            for q in range(KH // 8):
                ts = (j * (KH // 8) + q) % 2
                wait_guards("tensor", [trf_guard[ts]])
                for kk in range(8):
                    k = q * 8 + kk
                    last = kk == 7

                    def trf(e, j=j, k=k, ts=ts, kk=kk, last=last):
                        ins = e.transpose(
                            p_trf[ts][:, kk * 128:(kk + 1) * 128],
                            sRA[:, j * HID + k * 128:j * HID + (k + 1) * 128],
                            sIDB[:])
                        if last:
                            ins.then_inc(sem_pe.h, 1)
                    op("tensor", trf)
                sem_pe.inc()
                v_tr = sem_pe.v
                wait("scalar", sem_pe, v_tr)
                k0 = q * 8
                dst = bass.AP(sXN, k0 * TSP + j * 128,
                              [[sXN.shape[1], 128], [TSP, 8], [1, 128]])
                op("scalar", lambda e, dst=dst, ts=ts:
                   e.activation(dst, p_trf[ts][:, 0:1024],
                                AF.Copy, bias=0.0, scale=1.0)
                   .then_inc(sem_act.h, 1))
                trf_guard[ts] = (sem_act, sem_act.inc())
                if j == NJ - 1:
                    trf_quad_ev[q] = sem_act.v

        v_aggr0 = []
        rstd0 = {}
        plan_ln_stats_j(0, 0, v_aggr0)
        plan_ln_stats_j(0, 1, v_aggr0)
        rstd0[0] = plan_rstd_j(0, 0, v_aggr0[0])
        vx = {0: emit_xn_tm(0, rstd0[0])}
        plan_ln_stats_j(0, 2, v_aggr0)
        rstd0[1] = plan_rstd_j(0, 1, v_aggr0[1])
        vx[1] = emit_xn_tm(1, rstd0[1])
        plan_ln_stats_j(0, 3, v_aggr0)
        rstd0[2] = plan_rstd_j(0, 2, v_aggr0[2])
        vx[2] = emit_xn_tm(2, rstd0[2])
        rstd0[3] = plan_rstd_j(0, 3, v_aggr0[3])
        vx[3] = emit_xn_tm(3, rstd0[3])
        for j in range(NJ):
            emit_trf(j, vx[j])
        nrm = {0: (sem_act, sem_act.v)}
        v_aggr1 = None
        for sp in range(NSP):
            m1_evac_base = sem_act.v
            if sp == 0:
                def m1_fn(k):
                    return (sem_act, trf_quad_ev[min((k + 7) // 8, KH // 8 - 1)])
                plan_wmatmul(1, 64, KH, sXN, evac_m1, [], staged_gate=m1_fn)
            else:
                plan_wmatmul(1, 64, KH, sXN, evac_m1, [nrm[sp]])
            m1_xa_done = m1_evac_base + KI
            if sp == 0:
                m1_end0 = sem_pe.v
                for t in range(2 * NJ):
                    off = 1 + 2 * t if t < NJ else 9 + 4 * (t - NJ)
                    emit_x_load(t, "gpsimd", guards=[(sem_pe, m1_end0 + off)])
                for j in range(NJ):
                    emit_b3_accum(j, [(sem_xt[j], 16)])
            m2_dve_base = sem_dve.v
            plan_wmatmul(2, KI, KI, sRA, mk_evac_m2(m1_evac_base), [],
                         staged_gate=(sem_act, m1_evac_base, KI))
            if sp == 0:
                v_aggr1 = plan_ln_stats(1)
                for j in range(NJ):
                    emit_b3_accum(NJ + j, [(sem_xt[NJ + j], 16),
                                           (sem_dve, v_aggr1[j])])
                rstd1 = plan_ln_rstd_all(1, v_aggr1)
            plan_amatmul(3, False, mk_evac_resid(sp, False), [],
                         staged_gate=(sem_dve, m2_dve_base, KI))
            m4_act_base = sem_act.v
            m4_end = plan_wmatmul(4, KI, KH, sXN, mk_evac_relu(0, sRA), [])
            m5_act_base = None
            m5_in_gate = (sem_act, m4_act_base, KI)
            m5_hooks = {}
            if sp == 0:
                bc1 = [None]
                m5_hooks[4] = lambda: plan_ln_bcast_a(1, rstd1)
                def _bc1b():
                    bc1[0] = plan_ln_bcast_b(1)
                m5_hooks[8] = _bc1b
            m5_act_base = sem_act.v
            plan_wmatmul(5, KI, KI, sRA, mk_evac_relu(32, sRB), [],
                         staged_gate=m5_in_gate, hooks=m5_hooks)
            if sp == 0:
                emit_xbf_loads(1, "sync", guards=[(sem_pe, m4_end)])
            if sp == 0:
                def _hook():
                    nrm[1] = (sem_dve, plan_ln_norm(1, bc1[0]))
                plan_amatmul(6, False, mk_evac_resid(sp, True), [],
                             staged_gate=(sem_act, m5_act_base, KI),
                             hooks={8: _hook})
            else:
                plan_amatmul(6, False, mk_evac_resid(sp, True), [],
                             staged_gate=(sem_act, m5_act_base, KI))

        wait("gpsimd", sem_out, sem_out.v)

        block = ctx.enter_context(nc.Block())

        @block.sync
        def _(e):
            for fn in plan["sync"]:
                fn(e)

        @block.gpsimd
        def _(e):
            for fn in plan["gpsimd"]:
                fn(e)

        @block.tensor
        def _(e):
            for fn in plan["tensor"]:
                fn(e)

        @block.vector
        def _(e):
            for fn in plan["vector"]:
                fn(e)

        @block.scalar
        def _(e):
            for fn in plan["scalar"]:
                fn(e)

    return nc


def _stage_stationary(W, n_g, n_k):
    return np.ascontiguousarray(
        W.reshape(n_g, 128, n_k, 128).transpose(3, 0, 2, 1).reshape(128, n_g * n_k * 128)
    ).astype(BF)


def _stage_moving(W):
    return np.ascontiguousarray(
        W.reshape(4, 512, 32, 128).transpose(3, 0, 2, 1).reshape(128, 4 * 32 * 512)
    ).astype(BF)


def kernel(**inputs):
    X = np.asarray(inputs["L0_attn_hidden_states"], np.float32).reshape(SEQ, HID)
    W_in = np.asarray(inputs["L0_attn_W_in"], np.float32)
    conv_w = np.asarray(inputs["L0_attn_conv_w"], np.float32)
    W_out = np.asarray(inputs["L0_attn_W_out"], np.float32)
    D = np.asarray(inputs["L0_attn_D"], np.float32)
    W1 = np.asarray(inputs["L0_ffn_W1"], np.float32)
    b1 = np.asarray(inputs["L0_ffn_b1"], np.float32)
    W2 = np.asarray(inputs["L0_ffn_W2"], np.float32)
    b2 = np.asarray(inputs["L0_ffn_b2"], np.float32)
    W3 = np.asarray(inputs["L0_ffn_W3"], np.float32)
    b3 = np.asarray(inputs["L0_ffn_b3"], np.float32)
    _unused = (inputs["L0_attn_W_x"], inputs["L0_attn_W_dt"],
               inputs["L0_attn_b_dt"])
    del _unused

    W1S = _stage_stationary(W_in, 64, 16)
    W2S = _stage_stationary(conv_w, 32, 32)
    W3S = _stage_moving(W_out)
    W4S = _stage_stationary(W1, 32, 16)
    W5S = _stage_stationary(W2, 32, 32)
    W6S = _stage_moving(W3)
    consts = np.zeros((128, 97), np.float32)
    consts[:, 96] = 1e-5
    consts[:, 0:32] = b1.reshape(32, 128).T
    consts[:, 32:64] = b2.reshape(32, 128).T
    consts[:, 64:96] = D.reshape(32, 128).T
    identf = np.eye(128, dtype=np.float32)
    identb = identf.astype(BF)
    onesf = np.ones((1, 128), np.float32)
    bb3 = np.ascontiguousarray(np.broadcast_to(b3, (128, HID)))

    nc = build_program()
    in_maps = []
    for c in range(N_CORES):
        xc = np.ascontiguousarray(X[c * T_CORE:(c + 1) * T_CORE])
        xtok = xc.astype(BF)
        xbf = np.ascontiguousarray(
            xc.astype(BF).reshape(NSP, TSP, KH, 128).transpose(0, 2, 1, 3)
        ).reshape(KH * T_CORE, 128)
        in_maps.append({
            "X": xc, "XBF": xbf, "XTOK": xtok[:TSP],
            "W1S": W1S, "W2S": W2S, "W3S": W3S,
            "W4S": W4S, "W5S": W5S, "W6S": W6S,
            "CONST": consts, "IDF": identf, "IDB": identb, "ONF": onesf, "BB3": bb3,
        })
    res = run_bass_kernel_spmd(nc, in_maps, core_ids=list(range(N_CORES)))
    out = np.concatenate([np.asarray(r["OUT"]) for r in res.results], axis=0)
    return out.reshape(1, SEQ, HID).astype(np.float32)


# version 43
# speedup vs baseline: 1.0083x; 1.0021x over previous
import numpy as np
import ml_dtypes
from contextlib import ExitStack
import concourse.bass as bass
import concourse.mybir as mybir
from concourse.bass_utils import run_bass_kernel_spmd

F32 = mybir.dt.float32
BF16 = mybir.dt.bfloat16
AF = mybir.ActivationFunctionType
ALU = mybir.AluOpType
BF = ml_dtypes.bfloat16

HID = 2048
INTER = 4096
SEQ = 8192
N_CORES = 8
T_CORE = SEQ // N_CORES
NSP = 2
TSP = 512
NJ = TSP // 128
KH = HID // 128
KI = INTER // 128
NSLOT = 12
NBANK = 4


class Ctr:
    def __init__(self, name):
        self.name = name
        self.v = 0
        self.h = None

    def inc(self, n=1):
        self.v += n
        return self.v


def build_program():
    nc = bass.Bass("TRN2", target_bir_lowering=False)
    X = nc.dram_tensor("X", [T_CORE, HID], F32, kind="ExternalInput")
    W1S = nc.dram_tensor("W1S", [128, 64 * 16 * 128], BF16, kind="ExternalInput")
    W2S = nc.dram_tensor("W2S", [128, 32 * 32 * 128], BF16, kind="ExternalInput")
    W3S = nc.dram_tensor("W3S", [128, 4 * 32 * 512], BF16, kind="ExternalInput")
    W4S = nc.dram_tensor("W4S", [128, 32 * 16 * 128], BF16, kind="ExternalInput")
    W5S = nc.dram_tensor("W5S", [128, 32 * 32 * 128], BF16, kind="ExternalInput")
    W6S = nc.dram_tensor("W6S", [128, 4 * 32 * 512], BF16, kind="ExternalInput")
    CONST = nc.dram_tensor("CONST", [128, 97], F32, kind="ExternalInput")
    XBF = nc.dram_tensor("XBF", [KH * T_CORE, 128], BF16, kind="ExternalInput")
    XTOK = nc.dram_tensor("XTOK", [TSP, HID], BF16, kind="ExternalInput")
    ONF = nc.dram_tensor("ONF", [1, 128], F32, kind="ExternalInput")
    IDF = nc.dram_tensor("IDF", [128, 128], F32, kind="ExternalInput")
    IDB = nc.dram_tensor("IDB", [128, 128], BF16, kind="ExternalInput")
    BB3 = nc.dram_tensor("BB3", [128, HID], F32, kind="ExternalInput")
    OUT = nc.dram_tensor("OUT", [T_CORE, HID], F32, kind="ExternalOutput")

    wdram = {1: W1S, 2: W2S, 3: W3S, 4: W4S, 5: W5S, 6: W6S}

    plan = {k: [] for k in ("sync", "gpsimd", "tensor", "vector", "scalar")}

    def op(eng, fn):
        plan[eng].append(fn)

    def wait(eng, ctr, val):
        if val > 0:
            op(eng, lambda e, s=ctr, v=val: e.wait_ge(s.h, v))

    def wait_guards(eng, guards):
        agg = {}
        for g in guards:
            if g is None:
                continue
            s, v = g
            agg[s] = max(agg.get(s, 0), v)
        for s, v in agg.items():
            wait(eng, s, v)

    sem_ws = [Ctr(f"ws{i}") for i in range(NSLOT)]
    sem_xt = [Ctr(f"xt{i}") for i in range(NSP * NJ)]
    sem_c = [Ctr(f"c{i}") for i in range(4)]
    sem_xb = [Ctr(f"xb{i}") for i in range(NSP)]
    sem_tk = [Ctr(f"tk{i}") for i in range(NJ)]
    sem_pe = Ctr("pe")
    sem_act = Ctr("act")
    sem_dve = Ctr("dve")
    sem_out = Ctr("out")
    sem_warm = Ctr("warm")

    with ExitStack() as ctx:
        sX = ctx.enter_context(nc.sbuf_tensor("sX", [128, NSP * NJ * HID], F32))
        sXN = ctx.enter_context(nc.sbuf_tensor("sXN", [128, KH * TSP], BF16))
        sRA = ctx.enter_context(nc.sbuf_tensor("sRA", [128, KI * TSP], BF16))
        sRB = ctx.enter_context(nc.sbuf_tensor("sRB", [128, KI * TSP], BF16))
        sWr = ctx.enter_context(nc.sbuf_tensor("sWr", [128, NSLOT * 2048], BF16))
        sST = ctx.enter_context(nc.sbuf_tensor("sST", [128, NSP * NJ * 32], F32))
        sCN = ctx.enter_context(nc.sbuf_tensor("sCN", [128, 97], F32))
        sONF = ctx.enter_context(nc.sbuf_tensor("sONF", [1, 128], F32))
        sIDF = ctx.enter_context(nc.sbuf_tensor("sIDF", [128, 128], F32))
        sIDB = ctx.enter_context(nc.sbuf_tensor("sIDB", [128, 128], BF16))
        sMRm = ctx.enter_context(nc.sbuf_tensor("sMRm", [1, TSP], F32))
        sMRr = ctx.enter_context(nc.sbuf_tensor("sMRr", [1, TSP], F32))
        sMB = ctx.enter_context(nc.sbuf_tensor("sMB", [128, 2 * TSP], BF16))
        p_mm = [ctx.enter_context(nc.psum_tensor(f"pmm{b}", [128, 512], F32))
                for b in range(NBANK)]
        p_sta = ctx.enter_context(nc.psum_tensor("psta", [1, TSP], F32))
        p_stb = ctx.enter_context(nc.psum_tensor("pstb", [1, TSP], F32))
        p_trf = [ctx.enter_context(nc.psum_tensor(f"ptrf{t}", [128, 1024], BF16))
                 for t in range(2)]
        for c in sem_ws + sem_xt + sem_c + sem_xb + sem_tk + [sem_pe, sem_act, sem_dve, sem_out, sem_warm]:
            c.h = ctx.enter_context(nc.semaphore(c.name))

        ring_next = [0]
        ring_guard = [None] * NSLOT
        bank_guard = [None] * NBANK
        gidx = [0]

        def ring_alloc(n):
            base = ring_next[0]
            skipped = []
            if base + n > NSLOT:
                skipped = list(range(base, NSLOT))
                base = 0
            guards = [ring_guard[s] for s in range(base, base + n)] + \
                     [ring_guard[s] for s in skipped]
            ring_next[0] = (base + n) % NSLOT
            return base, guards

        def ring_set_guard(base, n, guard):
            for s in range(base, base + n):
                ring_guard[s] = guard

        def emit_wdma(phase, c0, ncols, base, guards, q="sync"):
            wait_guards(q, guards)
            dram = wdram[phase]
            ws = sem_ws[base]
            op(q, lambda e, d=dram, c0=c0, n=ncols, b=base, ws=ws:
               e.dma_start(sWr[:, b * 2048:b * 2048 + n], d[:, c0:c0 + n])
               .then_inc(ws.h, 16))
            return (ws, ws.inc(16))

        op("gpsimd", lambda e:
           e.dma_start(sMB[:, 0:128], XBF[0:128, :]).then_inc(sem_warm.h, 16))
        sem_warm.inc(16)

        for j in range(NJ):
            op("sync", lambda e, j=j:
               e.dma_start(sRA[:, (NJ + j) * HID:(NJ + j + 1) * HID],
                           XTOK[j * 128:(j + 1) * 128, :])
               .then_inc(sem_tk[j].h, 16))
            sem_tk[j].inc(16)

        op("sync", lambda e: e.dma_start(sCN[:], CONST[:]).then_inc(sem_c[0].h, 16))
        op("sync", lambda e: e.dma_start(sONF[:], ONF[:]).then_inc(sem_c[1].h, 16))
        op("sync", lambda e: e.dma_start(sIDF[:], IDF[:]).then_inc(sem_c[2].h, 16))
        op("sync", lambda e: e.dma_start(sIDB[:], IDB[:]).then_inc(sem_c[3].h, 16))
        for t in range(2):
            wait("sync", sem_tk[t], 16)
        for s in range(2, NSLOT):
            ring_guard[s] = (sem_tk[NJ - 1], 16)
        for c in sem_c:
            c.inc(16)

        def emit_xbf_loads(sp, q, guards=(), parts=4):
            xb = sem_xb[sp]
            wait_guards(q, guards)
            kc = KH // parts
            for p in range(parts):
                r0 = sp * KH * TSP + p * kc * TSP
                op(q, lambda e, r0=r0, p=p, kc=kc:
                   e.dma_start(sXN[:, p * kc * TSP:(p + 1) * kc * TSP],
                               XBF[r0:r0 + kc * TSP, :], transpose=True)
                   .then_inc(xb.h, 16))
                xb.inc(16)

        def emit_b3_accum(t, gates):
            wait_guards("gpsimd", gates)
            op("gpsimd", lambda e, t=t:
               e.dma_start(sX[:, t * HID:(t + 1) * HID], BB3[:],
                           accum_op=ALU.add)
               .then_inc(sem_xt[t].h, 16))
            sem_xt[t].inc(16)

        def emit_x_load(t, q, guards=()):
            wait_guards(q, guards)
            op(q, lambda e, t=t:
               e.dma_start(sX[:, t * HID:(t + 1) * HID], X[t * 128:t * 128 + 128, :])
               .then_inc(sem_xt[t].h, 16))
            sem_xt[t].inc(16)



        for eng in ("tensor", "scalar", "vector"):
            for c in sem_c:
                wait(eng, c, 16)

        def plan_ln_stats_j(sp, j, v_aggr):
            for _ in (0,):
                t = sp * NJ + j
                sb = t * 32
                if sp == 0:
                    wait("vector", sem_tk[j], 16)
                    src, c0 = sRA, (NJ + j) * HID
                else:
                    wait("vector", sem_xt[t], 16)
                    src, c0 = sX, t * HID
                for q in range(4):
                    op("vector", lambda e, src=src, c0=c0, sb=sb, q=q:
                       e.bn_stats(sST[:, sb + q * 6:sb + q * 6 + 6],
                                  src[:, c0 + q * 512:c0 + (q + 1) * 512])
                       .then_inc(sem_dve.h, 1))
                    sem_dve.inc()
                wait("vector", sem_dve, sem_dve.v)
                op("vector", lambda e, sb=sb:
                   e.bn_aggr(sST[:, sb + 24:sb + 26], sST[:, sb:sb + 24])
                   .then_inc(sem_dve.h, 1))
                v_aggr.append(sem_dve.inc())

        def plan_ln_stats(sp):
            v_aggr = []
            for j in range(NJ):
                plan_ln_stats_j(sp, j, v_aggr)
            return v_aggr

        def plan_rstd_j(sp, j, v_aggr_j):
            sb = (sp * NJ + j) * 32
            wait("scalar", sem_dve, v_aggr_j)
            op("scalar", lambda e, sb=sb:
               e.activation(sST[:, sb + 26:sb + 27], sST[:, sb + 25:sb + 26],
                            AF.Ln, bias=sCN[:, 96:97], scale=1.0)
               .then_inc(sem_act.h, 1))
            wait("scalar", sem_act, sem_act.inc())
            op("scalar", lambda e, sb=sb:
               e.activation(sST[:, sb + 27:sb + 28], sST[:, sb + 26:sb + 27],
                            AF.Exp, bias=0.0, scale=-0.5)
               .then_inc(sem_act.h, 1))
            return sem_act.inc()

        def plan_ln_rstd_all(sp, v_aggr):
            v_ln = {}
            out = {}

            def emit_ln(j):
                sb = (sp * NJ + j) * 32
                wait("scalar", sem_dve, v_aggr[j])
                op("scalar", lambda e, sb=sb:
                   e.activation(sST[:, sb + 26:sb + 27], sST[:, sb + 25:sb + 26],
                                AF.Ln, bias=sCN[:, 96:97], scale=1.0)
                   .then_inc(sem_act.h, 1))
                v_ln[j] = sem_act.inc()

            def emit_exp(j):
                sb = (sp * NJ + j) * 32
                wait("scalar", sem_act, v_ln[j])
                op("scalar", lambda e, sb=sb:
                   e.activation(sST[:, sb + 27:sb + 28], sST[:, sb + 26:sb + 27],
                                AF.Exp, bias=0.0, scale=-0.5)
                   .then_inc(sem_act.h, 1))
                out[j] = sem_act.inc()

            emit_ln(0)
            for j in range(1, NJ):
                emit_ln(j)
                emit_exp(j - 1)
            emit_exp(NJ - 1)
            return [out[j] for j in range(NJ)]

        bc_state = {}

        def plan_ln_bcast_a(sp, v_rstd, guards_pe=(), guards_dve=()):
            wait_guards("tensor", list(guards_pe))
            for j in range(NJ):
                t = sp * NJ + j
                sb = t * 32
                wait("tensor", sem_act, v_rstd[j])
                op("tensor", lambda e, sb=sb, j=j:
                   e.transpose(p_sta[0:1, j * 128:(j + 1) * 128],
                               sST[:, sb + 24:sb + 25], sIDF[:]))
                op("tensor", lambda e, sb=sb, j=j:
                   e.transpose(p_stb[0:1, j * 128:(j + 1) * 128],
                               sST[:, sb + 27:sb + 28], sIDF[:])
                   .then_inc(sem_pe.h, 1))
                sem_pe.inc()
            v_tr = sem_pe.v
            wait("vector", sem_pe, v_tr)
            wait_guards("vector", list(guards_dve))
            op("vector", lambda e: e.tensor_copy(sMRm[:], p_sta[0:1, :])
               .then_inc(sem_dve.h, 1))
            sem_dve.inc()
            op("vector", lambda e: e.tensor_copy(sMRr[:], p_stb[0:1, :])
               .then_inc(sem_dve.h, 1))
            v_rows = sem_dve.inc()
            wait("vector", sem_dve, v_rows)
            op("vector", lambda e: e.tensor_mul(sMRm[:], sMRm[:], sMRr[:])
               .then_inc(sem_dve.h, 1))
            v_mr = sem_dve.inc()
            bc_state[sp] = (v_mr, v_rows)

        def plan_ln_bcast_b(sp):
            v_mr, v_rows = bc_state[sp]
            for src, dstcol, gate in ((sMRm, 0, v_mr), (sMRr, TSP, v_rows)):
                b = gidx[0] % NBANK
                wait_guards("tensor", [bank_guard[b]])
                wait("tensor", sem_dve, gate)
                op("tensor", lambda e, b=b, src=src:
                   e.matmul(p_mm[b][:], sONF[:], src[:], start=True, stop=True)
                   .then_inc(sem_pe.h, 1))
                v_g = sem_pe.inc()
                gidx[0] += 1
                wait("vector", sem_pe, v_g)
                op("vector", lambda e, b=b, dstcol=dstcol:
                   e.tensor_copy(sMB[:, dstcol:dstcol + TSP], p_mm[b][:])
                   .then_inc(sem_dve.h, 1))
                bank_guard[b] = (sem_dve, sem_dve.inc())
            return sem_dve.v

        def plan_ln_norm(sp, v_bc, chunk_gates=None):
            if chunk_gates is None:
                wait("vector", sem_xb[sp], sem_xb[sp].v)
            wait("vector", sem_dve, v_bc)
            v_mul = []
            for k in range(KH):
                if chunk_gates is not None:
                    wait_guards("vector", [chunk_gates[k]])
                op("vector", lambda e, k=k:
                   e.tensor_tensor(sXN[:, k * TSP:(k + 1) * TSP],
                                   sXN[:, k * TSP:(k + 1) * TSP],
                                   sMB[:, TSP:2 * TSP], ALU.mult)
                   .then_inc(sem_dve.h, 1))
                v_mul.append(sem_dve.inc())
            for k in range(KH):
                wait("vector", sem_dve, v_mul[k])
                op("vector", lambda e, k=k:
                   e.tensor_tensor(sXN[:, k * TSP:(k + 1) * TSP],
                                   sXN[:, k * TSP:(k + 1) * TSP],
                                   sMB[:, 0:TSP], ALU.subtract)
                   .then_inc(sem_dve.h, 1))
                sem_dve.inc()
            return sem_dve.v

        def plan_wmatmul(phase, n_g, n_k, rhs_tensor, evac, first_gates,
                         staged_gate=None, hooks=None, early_q=0):
            hooks = hooks or {}
            nslots = (n_k * 128) // 2048
            for g in range(n_g):
                base, guards = ring_alloc(nslots)
                wv = emit_wdma(phase, g * n_k * 128, n_k * 128, base, guards,
                               q=("gpsimd" if g < early_q else "sync"))
                b = gidx[0] % NBANK
                if g == 0:
                    wait_guards("tensor", first_gates)
                wait_guards("tensor", [bank_guard[b]])
                wait_guards("tensor", [wv])
                for k in range(n_k):
                    if g == 0 and staged_gate is not None and k % 8 == 0:
                        if callable(staged_gate):
                            gd = staged_gate(k)
                            if gd is not None:
                                wait("tensor", gd[0], gd[1])
                        else:
                            ctr, gbase, tot = staged_gate
                            wait("tensor", ctr, gbase + min(k + 8, tot))
                    def mm(e, b=b, base=base, k=k, last=(k == n_k - 1), n_k=n_k):
                        ins = e.matmul(
                            p_mm[b][:],
                            sWr[:, base * 2048 + k * 128:base * 2048 + (k + 1) * 128],
                            rhs_tensor[:, k * TSP:(k + 1) * TSP],
                            start=(k == 0), stop=last)
                        if last:
                            ins.then_inc(sem_pe.h, 1)
                    op("tensor", mm)
                v_grp = sem_pe.inc()
                gidx[0] += 1
                ring_set_guard(base, nslots, (sem_pe, v_grp))
                bank_guard[b] = evac(g, b, v_grp)
                if g + 1 in hooks:
                    hooks[g + 1]()
            return sem_pe.v

        amm_staged_gate = [None]

        def plan_amatmul(phase, with_b3, evac, first_gates, hooks=None,
                         staged_gate=None):
            hooks = hooks or {}
            amm_staged_gate[0] = staged_gate
            ngroup = 0
            for gp in range(4):
                qinfo = []
                for q in range(4):
                    base, guards = ring_alloc(2)
                    wv = emit_wdma(phase, gp * 16384 + q * 4096, 4096, base, guards)
                    qinfo.append((base, wv))
                for j in range(NJ):
                    b = gidx[0] % NBANK
                    if gp == 0 and j == 0:
                        wait_guards("tensor", first_gates)
                        sg = amm_staged_gate[0]
                    else:
                        sg = None
                    wait_guards("tensor", [bank_guard[b]])
                    last_j = (j == NJ - 1)
                    for k in range(KI):
                        if k % 8 == 0 and j == 0:
                            wait_guards("tensor", [qinfo[k // 8][1]])
                        if sg is not None and k % 16 == 0:
                            ctr, gbase, tot = sg
                            wait("tensor", ctr, gbase + min(k + 16, tot))
                        base = qinfo[k // 8][0]
                        kk = k % 8
                        inc = (k == KI - 1) or (last_j and kk == 7)

                        def mm(e, b=b, base=base, kk=kk, k=k, j=j,
                               st=(k == 0), last=(k == KI - 1),
                               inc=inc):
                            ins = e.matmul(
                                p_mm[b][:],
                                stat_sRB[:, k * TSP + j * 128:k * TSP + (j + 1) * 128],
                                sWr[:, base * 2048 + kk * 512:base * 2048 + (kk + 1) * 512],
                                start=st, stop=last)
                            if inc:
                                ins.then_inc(sem_pe.h, 1)
                        op("tensor", mm)
                        if inc:
                            v_now = sem_pe.inc()
                            if last_j:
                                ring_set_guard(qinfo[k // 8][0], 2, (sem_pe, v_now))
                    v_grp = sem_pe.v
                    gidx[0] += 1
                    bank_guard[b] = evac(gp, j, b, v_grp)
                    ngroup += 1
                    if ngroup in hooks:
                        hooks[ngroup]()

        stat_sRB = sRB

        def evac_m1(g, b, v_grp):
            wait("scalar", sem_pe, v_grp)
            if g < KI:
                op("scalar", lambda e, g=g, b=b:
                   e.activation(sRA[:, g * TSP:(g + 1) * TSP], p_mm[b][:],
                                AF.Copy, bias=0.0, scale=1.0)
                   .then_inc(sem_act.h, 1))
            else:
                op("scalar", lambda e, g=g, b=b:
                   e.activation(sRB[:, (g - KI) * TSP:(g - KI + 1) * TSP], p_mm[b][:],
                                AF.Silu, bias=0.0, scale=1.0)
                   .then_inc(sem_act.h, 1))
            return (sem_act, sem_act.inc())

        def mk_evac_m2(m1_evac_base):
            def evac_m2(g, b, v_grp):
                wait("vector", sem_pe, v_grp)
                wait("vector", sem_act, m1_evac_base + KI + g + 1)
                op("vector", lambda e, g=g, b=b:
                   e.scalar_tensor_tensor(sRB[:, g * TSP:(g + 1) * TSP], p_mm[b][:],
                                          sCN[:, 64 + g:65 + g],
                                          sRB[:, g * TSP:(g + 1) * TSP],
                                          ALU.mult, ALU.mult)
                   .then_inc(sem_dve.h, 1))
                return (sem_dve, sem_dve.inc())
            return evac_m2

        def mk_evac_relu(bcol, dst):
            def evac_relu(g, b, v_grp):
                wait("scalar", sem_pe, v_grp)
                op("scalar", lambda e, g=g, b=b, bcol=bcol, dst=dst:
                   e.activation(dst[:, g * TSP:(g + 1) * TSP], p_mm[b][:],
                                AF.Relu, bias=sCN[:, bcol + g:bcol + g + 1], scale=1.0)
                   .then_inc(sem_act.h, 1))
                return (sem_act, sem_act.inc())
            return evac_relu

        def mk_evac_resid(sp, store):
            def evac_resid(gp, j, b, v_grp):
                t = sp * NJ + j
                col = t * HID + gp * 512
                if gp == 0:
                    wait("vector", sem_xt[t], 32)
                wait("vector", sem_pe, v_grp)
                op("vector", lambda e, col=col, b=b:
                   e.tensor_tensor(sX[:, col:col + 512], p_mm[b][:],
                                   sX[:, col:col + 512], ALU.add)
                   .then_inc(sem_dve.h, 1))
                v = sem_dve.inc()
                if store:
                    oq = "sync" if (sp == 1 and gp == 3) else "gpsimd"
                    wait(oq, sem_dve, v)
                    op(oq, lambda e, t=t, gp=gp, col=col:
                       e.dma_start(OUT[t * 128:t * 128 + 128, gp * 512:(gp + 1) * 512],
                                   sX[:, col:col + 512])
                       .then_inc(sem_out.h, 16))
                    sem_out.inc(16)
                return (sem_dve, v)
            return evac_resid

        trf_guard = [None, None]
        trf_quad_ev = {}

        def emit_xn_tm(j, v_rstd):
            sb = j * 32
            wait("vector", sem_act, v_rstd)
            op("vector", lambda e, j=j, sb=sb:
               e.tensor_scalar(sRA[:, j * HID:(j + 1) * HID],
                               sRA[:, (NJ + j) * HID:(NJ + j + 1) * HID],
                               sST[:, sb + 24:sb + 25],
                               sST[:, sb + 27:sb + 28],
                               ALU.subtract, ALU.mult)
               .then_inc(sem_dve.h, 1))
            return sem_dve.inc()

        def emit_trf(j, v_xntm):
            wait("tensor", sem_dve, v_xntm)
            for q in range(KH // 8):
                ts = (j * (KH // 8) + q) % 2
                wait_guards("tensor", [trf_guard[ts]])
                for kk in range(8):
                    k = q * 8 + kk
                    last = kk == 7

                    def trf(e, j=j, k=k, ts=ts, kk=kk, last=last):
                        ins = e.transpose(
                            p_trf[ts][:, kk * 128:(kk + 1) * 128],
                            sRA[:, j * HID + k * 128:j * HID + (k + 1) * 128],
                            sIDB[:])
                        if last:
                            ins.then_inc(sem_pe.h, 1)
                    op("tensor", trf)
                sem_pe.inc()
                v_tr = sem_pe.v
                wait("scalar", sem_pe, v_tr)
                k0 = q * 8
                dst = bass.AP(sXN, k0 * TSP + j * 128,
                              [[sXN.shape[1], 128], [TSP, 8], [1, 128]])
                op("scalar", lambda e, dst=dst, ts=ts:
                   e.activation(dst, p_trf[ts][:, 0:1024],
                                AF.Copy, bias=0.0, scale=1.0)
                   .then_inc(sem_act.h, 1))
                trf_guard[ts] = (sem_act, sem_act.inc())
                if j == NJ - 1:
                    trf_quad_ev[q] = sem_act.v

        v_aggr0 = []
        rstd0 = {}
        plan_ln_stats_j(0, 0, v_aggr0)
        plan_ln_stats_j(0, 1, v_aggr0)
        rstd0[0] = plan_rstd_j(0, 0, v_aggr0[0])
        vx = {0: emit_xn_tm(0, rstd0[0])}
        plan_ln_stats_j(0, 2, v_aggr0)
        rstd0[1] = plan_rstd_j(0, 1, v_aggr0[1])
        vx[1] = emit_xn_tm(1, rstd0[1])
        plan_ln_stats_j(0, 3, v_aggr0)
        rstd0[2] = plan_rstd_j(0, 2, v_aggr0[2])
        vx[2] = emit_xn_tm(2, rstd0[2])
        rstd0[3] = plan_rstd_j(0, 3, v_aggr0[3])
        vx[3] = emit_xn_tm(3, rstd0[3])
        for j in range(NJ):
            emit_trf(j, vx[j])
        nrm = {0: (sem_act, sem_act.v)}
        v_aggr1 = None
        for sp in range(NSP):
            m1_evac_base = sem_act.v
            if sp == 0:
                def m1_fn(k):
                    return (sem_act, trf_quad_ev[min((k + 7) // 8, KH // 8 - 1)])
                plan_wmatmul(1, 64, KH, sXN, evac_m1, [], staged_gate=m1_fn)
            else:
                plan_wmatmul(1, 64, KH, sXN, evac_m1, [nrm[sp]])
            m1_xa_done = m1_evac_base + KI
            if sp == 0:
                m1_end0 = sem_pe.v
                for t in range(2 * NJ):
                    off = 1 + 2 * t if t < NJ else 9 + 4 * (t - NJ)
                    emit_x_load(t, "gpsimd", guards=[(sem_pe, m1_end0 + off)])
                for j in range(NJ):
                    emit_b3_accum(j, [(sem_xt[j], 16)])
            m2_dve_base = sem_dve.v
            plan_wmatmul(2, KI, KI, sRA, mk_evac_m2(m1_evac_base), [],
                         staged_gate=(sem_act, m1_evac_base, KI))
            if sp == 0:
                v_aggr1 = plan_ln_stats(1)
                for j in range(NJ):
                    emit_b3_accum(NJ + j, [(sem_xt[NJ + j], 16),
                                           (sem_dve, v_aggr1[j])])
                rstd1 = plan_ln_rstd_all(1, v_aggr1)
            plan_amatmul(3, False, mk_evac_resid(sp, False), [],
                         staged_gate=(sem_dve, m2_dve_base, KI))
            m4_act_base = sem_act.v
            m4_end = plan_wmatmul(4, KI, KH, sXN, mk_evac_relu(0, sRA), [])
            m5_act_base = None
            m5_in_gate = (sem_act, m4_act_base, KI)
            m5_hooks = {}
            if sp == 0:
                bc1 = [None]
                m5_hooks[4] = lambda: plan_ln_bcast_a(1, rstd1)
                def _bc1b():
                    bc1[0] = plan_ln_bcast_b(1)
                m5_hooks[8] = _bc1b
            m5_act_base = sem_act.v
            plan_wmatmul(5, KI, KI, sRA, mk_evac_relu(32, sRB), [],
                         staged_gate=m5_in_gate, hooks=m5_hooks)
            if sp == 0:
                emit_xbf_loads(1, "sync", guards=[(sem_pe, m4_end)])
            if sp == 0:
                def _hook():
                    nrm[1] = (sem_dve, plan_ln_norm(1, bc1[0]))
                plan_amatmul(6, False, mk_evac_resid(sp, True), [],
                             staged_gate=(sem_act, m5_act_base, KI),
                             hooks={8: _hook})
            else:
                plan_amatmul(6, False, mk_evac_resid(sp, True), [],
                             staged_gate=(sem_act, m5_act_base, KI))

        wait("gpsimd", sem_out, sem_out.v)

        block = ctx.enter_context(nc.Block())

        @block.sync
        def _(e):
            for fn in plan["sync"]:
                fn(e)

        @block.gpsimd
        def _(e):
            for fn in plan["gpsimd"]:
                fn(e)

        @block.tensor
        def _(e):
            for fn in plan["tensor"]:
                fn(e)

        @block.vector
        def _(e):
            for fn in plan["vector"]:
                fn(e)

        @block.scalar
        def _(e):
            for fn in plan["scalar"]:
                fn(e)

    return nc


def _stage_stationary(W, n_g, n_k):
    return np.ascontiguousarray(
        W.reshape(n_g, 128, n_k, 128).transpose(3, 0, 2, 1).reshape(128, n_g * n_k * 128)
    ).astype(BF)


def _stage_moving(W):
    return np.ascontiguousarray(
        W.reshape(4, 512, 32, 128).transpose(3, 0, 2, 1).reshape(128, 4 * 32 * 512)
    ).astype(BF)


def kernel(**inputs):
    X = np.asarray(inputs["L0_attn_hidden_states"], np.float32).reshape(SEQ, HID)
    W_in = np.asarray(inputs["L0_attn_W_in"], np.float32)
    conv_w = np.asarray(inputs["L0_attn_conv_w"], np.float32)
    W_out = np.asarray(inputs["L0_attn_W_out"], np.float32)
    D = np.asarray(inputs["L0_attn_D"], np.float32)
    W1 = np.asarray(inputs["L0_ffn_W1"], np.float32)
    b1 = np.asarray(inputs["L0_ffn_b1"], np.float32)
    W2 = np.asarray(inputs["L0_ffn_W2"], np.float32)
    b2 = np.asarray(inputs["L0_ffn_b2"], np.float32)
    W3 = np.asarray(inputs["L0_ffn_W3"], np.float32)
    b3 = np.asarray(inputs["L0_ffn_b3"], np.float32)
    _unused = (inputs["L0_attn_W_x"], inputs["L0_attn_W_dt"],
               inputs["L0_attn_b_dt"])
    del _unused

    W1S = _stage_stationary(W_in, 64, 16)
    W2S = _stage_stationary(conv_w, 32, 32)
    W3S = _stage_moving(W_out)
    W4S = _stage_stationary(W1, 32, 16)
    W5S = _stage_stationary(W2, 32, 32)
    W6S = _stage_moving(W3)
    consts = np.zeros((128, 97), np.float32)
    consts[:, 96] = 1e-5
    consts[:, 0:32] = b1.reshape(32, 128).T
    consts[:, 32:64] = b2.reshape(32, 128).T
    consts[:, 64:96] = D.reshape(32, 128).T
    identf = np.eye(128, dtype=np.float32)
    identb = identf.astype(BF)
    onesf = np.ones((1, 128), np.float32)
    bb3 = np.ascontiguousarray(np.broadcast_to(b3, (128, HID)))

    nc = build_program()
    in_maps = []
    for c in range(N_CORES):
        xc = np.ascontiguousarray(X[c * T_CORE:(c + 1) * T_CORE])
        xtok = xc.astype(BF)
        xbf = np.ascontiguousarray(
            xc.astype(BF).reshape(NSP, TSP, KH, 128).transpose(0, 2, 1, 3)
        ).reshape(KH * T_CORE, 128)
        in_maps.append({
            "X": xc, "XBF": xbf, "XTOK": xtok[:TSP],
            "W1S": W1S, "W2S": W2S, "W3S": W3S,
            "W4S": W4S, "W5S": W5S, "W6S": W6S,
            "CONST": consts, "IDF": identf, "IDB": identb, "ONF": onesf, "BB3": bb3,
        })
    res = run_bass_kernel_spmd(nc, in_maps, core_ids=list(range(N_CORES)))
    out = np.concatenate([np.asarray(r["OUT"]) for r in res.results], axis=0)
    return out.reshape(1, SEQ, HID).astype(np.float32)
